# version 04 (final)
# baseline (speedup 1.0000x reference)
import math

import numpy as np
import ml_dtypes

import concourse.bass as bass
import concourse.mybir as mybir
from concourse.bass_utils import run_bass_kernel_spmd

F32 = mybir.dt.float32
BF16 = mybir.dt.bfloat16
AF = mybir.ActivationFunctionType
ALU = mybir.AluOpType
AX = mybir.AxisListType
ET = mybir.EngineType

B, S, D = 4, 2048, 2048
T = 1024
NT = T // 128
NC = D // 128
NKT = S // 128
LAT = 512
NH = 16
NKV = 8
HD = 128
RD = 64
I_FF = 1024
TOPK_KEYS = 819.0
EPS = 1e-5
SCALE_QK = 1.0 / math.sqrt(128.0)
NEG = -30000.0
N_BISECT = 36

ENGINES = ("PE", "ACT", "DVE", "POOL", "SP")
ENG_ATTR = {"PE": "tensor", "ACT": "scalar", "DVE": "vector",
            "POOL": "gpsimd", "SP": "sync"}


class Ev:
    __slots__ = ("semid", "val", "eng")

    def __init__(self, semid, val, eng=None):
        self.semid = semid
        self.val = val
        self.eng = eng


class EM:
    def __init__(self, nc):
        self.nc = nc
        self.thunks = {e: [] for e in ENGINES}
        self.ecount = {e: 0 for e in ENGINES}
        self.tags = {}
        self.regions = {}
        self.barrier = {}
        self.wm = {e: {} for e in ENGINES}
        self.semmap = {}

    def _sem(self, semid):
        if semid not in self.semmap:
            self.semmap[semid] = self.nc.alloc_semaphore(semid)
        return self.semmap[semid]

    def _reg(self, key):
        r = self.regions.get(key)
        if r is None:
            r = {"w": None, "r": list(self.barrier.get(key[0], ()))}
            self.regions[key] = r
        return r

    def _conf(self, key):
        grp, sub = key
        if sub == "*":
            return [k for k in self.regions if k[0] == grp]
        out = []
        for k in self.regions:
            if k[0] == grp and (k[1] == "*" or k[1] == sub):
                out.append(k)
        return out

    def switch_tenant(self, grp):
        evs = []
        for k in [k for k in self.regions if k[0] == grp]:
            r = self.regions.pop(k)
            if r["w"] is not None:
                evs.append(r["w"])
            evs.extend(r["r"])
        self.barrier[grp] = evs

    def _needs(self, engine, reads, writes):
        need = {}

        def add(ev):
            if ev is None:
                return
            if ev.eng == engine and engine == "PE":
                return
            cur = need.get(ev.semid)
            if cur is None or cur < ev.val:
                need[ev.semid] = ev.val

        for key in reads:
            for k2 in self._conf(key):
                add(self.regions[k2]["w"])
            self._reg(key)
        for key in writes:
            for k2 in self._conf(key):
                r = self.regions[k2]
                add(r["w"])
                for ev in r["r"]:
                    if ev.eng == engine:
                        continue
                    add(ev)
            self._reg(key)
        wm = self.wm[engine]
        out = []
        for sid, val in need.items():
            if wm.get(sid, -1) >= val:
                continue
            out.append((sid, val))
            wm[sid] = val
        return out

    def _record(self, ev, reads, writes):
        for key in reads:
            self._reg(key)["r"].append(ev)
        for key in writes:
            r = self._reg(key)
            r["w"] = ev
            r["r"] = []

    def op(self, engine, reads, writes, fn):
        waits = self._needs(engine, reads, writes)
        self.ecount[engine] += 1
        cnt = self.ecount[engine]
        sid = "e_" + engine
        self.thunks[engine].append(("op", waits, fn, sid))
        ev = Ev(sid, cnt, engine)
        self._record(ev, reads, writes)
        return ev

    def dma(self, queue, reads, writes, tag, fn, inc=16):
        waits = self._needs(queue, reads, writes)
        n = self.tags.get(tag, 0) + 1
        self.tags[tag] = n
        sid = "d_" + tag
        self.thunks[queue].append(("dma", waits, fn, sid, inc))
        ev = Ev(sid, inc * n, None)
        self._record(ev, reads, writes)
        return ev

    def raw(self, engine, fn):
        self.thunks[engine].append(("raw", fn))

    def wait(self, engine, ev):
        wm = self.wm[engine]
        if wm.get(ev.semid, -1) >= ev.val:
            return
        wm[ev.semid] = ev.val
        self.thunks[engine].append(("waitonly", [(ev.semid, ev.val)]))

    def barrier_tag(self, engine, tag):
        n = self.tags.get(tag, 0)
        if n:
            self.wait(engine, Ev("d_" + tag, 16 * n, None))

    def finalize(self, block):
        for e in ENGINES:
            if not self.thunks[e]:
                continue

            def make(e):
                def body(eng):
                    for th in self.thunks[e]:
                        kind = th[0]
                        if kind == "raw":
                            th[1](eng)
                        elif kind == "waitonly":
                            for sid, val in th[1]:
                                eng.wait_ge(self._sem(sid), val)
                        elif kind == "op":
                            _, waits, fn, sid = th
                            for s, v in waits:
                                eng.wait_ge(self._sem(s), v)
                            fn(eng).then_inc(self._sem(sid), 1)
                        else:
                            _, waits, fn, sid, inc = th
                            for s, v in waits:
                                eng.wait_ge(self._sem(s), v)
                            try:
                                fn(eng).then_inc(self._sem(sid), inc)
                            except Exception as ex:
                                cv = {}
                                if fn.__closure__:
                                    cv = dict(zip(fn.__code__.co_freevars,
                                                  [c.cell_contents if not callable(getattr(c, 'cell_contents', None)) else '<fn>' for c in fn.__closure__]))
                                raise RuntimeError(
                                    f"DMA thunk failed sid={sid} "
                                    f"defaults={fn.__defaults__} free={list(cv)[:6]}: {ex}") from ex
                return body
            getattr(block, ENG_ATTR[e])(make(e))


def _rope_tables(parity):
    pos = np.concatenate(
        [np.arange(128, dtype=np.float64) + (2 * j + parity) * 128
         for j in range(NT)])

    def make(dim, theta):
        half = dim // 2
        inv = 1.0 / (theta ** (np.arange(0, dim, 2, dtype=np.float64) / dim))
        ang = np.outer(pos, inv)
        cos_t = np.empty((128, T), np.float32)
        sin_t = np.empty((128, T), np.float32)
        for p in range(128):
            d = p % dim
            a = ang[:, d % half]
            cos_t[p] = np.cos(a).astype(np.float32)
            s = np.sin(a).astype(np.float32)
            sin_t[p] = -s if d < half else s
        return cos_t, sin_t

    mc, ms = make(RD, 10000.0)
    sc, ss = make(HD, 1000000.0)
    return mc, ms, sc, ss


def _causal_masks(parity):
    tri = np.where(np.arange(128)[:, None] >= np.arange(128)[None, :],
                   0.0, NEG).astype(np.float32)
    zero = np.zeros((128, 128), np.float32)
    neg = np.full((128, 128), NEG, np.float32)
    return (tri, neg) if parity == 0 else (zero, tri)


def build_nc(taps=(), phases=(1, 2, 3), stop_after=None):
    taps = set(taps)
    nc = bass.Bass("TRN2", target_bir_lowering=False, num_devices=8)
    em = EM(nc)

    def din(name, shape, dt=F32):
        return nc.dram_tensor(name, list(shape), dt, kind="ExternalInput").ap()

    x_in = din("x", (NC, 128, T))
    out = nc.dram_tensor("out", [NC, 128, T], F32, kind="ExternalOutput").ap()

    W = {}
    for nm, shp in [
        ("attn_Wq", (D, D)), ("attn_Wk", (D, 1024)), ("attn_Wv", (D, 1024)),
        ("attn_Wo", (D, D)), ("attn_Wr", (D, 16)),
        ("ffn_kv_d_w", (D, LAT)), ("ffn_q_d_w", (D, LAT)),
        ("ffn_k_u_w", (LAT, 1024)), ("ffn_v_u_w", (LAT, D)),
        ("ffn_q_u_w", (LAT, 1024)), ("ffn_rope_k_w", (D, 1024)),
        ("ffn_rope_q_w", (LAT, 1024)), ("ffn_o_w", (D, D)),
        ("ffn_sh_gate", (D, I_FF)), ("ffn_sh_up", (D, I_FF)),
        ("ffn_sh_down", (I_FF, D)),
        ("ffn_r_gate", (7, D, I_FF)), ("ffn_r_up", (7, D, I_FF)),
        ("ffn_r_down", (7, I_FF, D)),
        ("ffn_router_w", (D, 7)),
    ]:
        W[nm] = din(nm, shp)
    Wb = {}
    for nm, shp in [
        ("attn_bq", (1, D)), ("attn_bk", (1, 1024)), ("attn_bv", (1, 1024)),
        ("attn_bo", (1, D)), ("attn_br", (16, 1)), ("ffn_router_b", (7, 1)),
        ("ffn_ln1_w", (128, NC)), ("ffn_ln2_w", (128, NC)),
    ]:
        Wb[nm] = din(nm, shp)
    mla_cos_in = din("mla_cos", (128, T))
    mla_sin_in = din("mla_sin", (128, T))
    sp_cos_in = din("sp_cos", (128, T))
    sp_sin_in = din("sp_sin", (128, T))
    m0_in = din("m0", (128, 128))
    m1_in = din("m1", (128, 128))
    ident_bf_in = din("ident_bf", (128, 128))
    ident_f_in = din("ident_f", (128, 128))

    cc_in = nc.dram_tensor("cc_in", [4096, T], BF16, kind="Internal").ap()
    cc_out = nc.dram_tensor("cc_out", [8 * 4096, T], BF16, kind="Internal",
                            addr_space="Shared").ap()
    cc2_in = nc.dram_tensor("cc2_in", [2048, T], BF16, kind="Internal").ap()
    cc2_out = nc.dram_tensor("cc2_out", [8 * 2048, T], BF16, kind="Internal",
                             addr_space="Shared").ap()
    kv_pair = nc.dram_tensor("kv_pair", [8192, T], BF16,
                             kind="Internal").ap()
    kv2_pair = nc.dram_tensor("kv2_pair", [4096, T], BF16,
                              kind="Internal").ap()
    rt_pair = nc.dram_tensor("rt_pair", [32, T], F32, kind="Internal").ap()
    rt_in = nc.dram_tensor("rt_in", [16, T], F32, kind="Internal").ap()
    rt_out = nc.dram_tensor("rt_out", [8 * 16, T], F32, kind="Internal",
                            addr_space="Shared").ap()
    x_spill = nc.dram_tensor("x_spill", [NC, 128, T], F32, kind="Internal").ap()
    hs_spill = nc.dram_tensor("hs_spill", [NC, 128, T], F32,
                              kind="Internal").ap()
    mask_dram = nc.dram_tensor("mask_dram", [16, S], BF16, kind="Internal").ap()

    alloc = nc.alloc_sbuf_tensor_at
    SB_BASE = 16512
    pos = [SB_BASE]

    def place(name, shape, dt):
        nbytes = shape[-1] * mybir.dt.size(dt)
        at = pos[0]
        pos[0] = at + ((nbytes + 31) // 32) * 32
        return alloc(name, list(shape), dt, offset=at)

    def begin(at):
        pos[0] = at

    XG = SB_BASE
    begin(XG)
    Xc = [place(f"x{c}", [128, T], F32) for c in range(NC)]
    X_END = pos[0]
    assert X_END - XG == 65536
    begin(XG)
    K_sl = [place(f"ksl{i}", [128, S], BF16) for i in range(2)]
    Vt_sl = [place(f"vtsl{i}", [128, S], BF16) for i in range(2)]
    PRB = [place(f"prb{i}", [128, S], BF16) for i in range(2)]
    PRBT = place("prbt", [128, S], BF16)
    Q_sl = [place(f"qsl{i}", [128, T], BF16) for i in range(2)]
    RAWA = [place(f"rawa{i}", [128, T], F32) for i in range(2)]
    QD = [place(f"qd{c}", [128, T], BF16) for c in range(4)]
    KVD = [place(f"kvd{c}", [128, T], BF16) for c in range(4)]
    VF = PRB[0]
    assert pos[0] - XG <= 65536, pos[0] - XG

    begin(X_END)
    HSG = pos[0]
    HSc = [place(f"hs{c}", [128, T], F32) for c in range(NC)]
    HS_END = pos[0]
    assert HS_END - HSG == 65536
    begin(HSG)
    QAll = [place(f"qall{h}", [128, T], BF16) for h in range(NH)]
    R16 = place("r16", [16, 6144], F32)
    mbc0_off = pos[0]
    MBC = [place(f"mbc{i}", [128, S], BF16) for i in range(2)]
    M16 = alloc("m16", [16, T], F32, offset=mbc0_off)
    assert pos[0] - HSG <= 65536, pos[0] - HSG
    begin(HSG)
    HE = [[place(f"he{s}_{c}", [128, T], BF16) for c in range(8)]
          for s in range(2)]
    HRC = [place(f"hrc{i}", [128, 512], F32) for i in range(2)]
    XRC = [place(f"xrc{i}", [128, 512], F32) for i in range(2)]
    assert pos[0] - HSG <= 65536, pos[0] - HSG

    begin(HS_END)
    XNG = pos[0]
    XNc = [place(f"xn{c}", [128, T], BF16) for c in range(NC)]
    XN_END = pos[0]
    AOc = XNc

    begin(XN_END)
    WFG = pos[0]
    TBa = place("tba", [128, T], BF16)
    TBb = place("tbb", [128, T], BF16)
    aba_off = pos[0]
    ABa = place("aba", [128, T], F32)
    abb_off = pos[0]
    ABb = place("abb", [128, T], F32)
    WF_END = pos[0]

    NSLOT = 4
    WS = [place(f"ws{i}", [128, 2048], BF16) for i in range(NSLOT)]
    RING_END = pos[0]

    G5G = pos[0]
    BIAS = place("biasrow", [1, D], BF16)
    P7sb = alloc("p7sb", [7, T], F32, offset=G5G)
    ROWb = alloc("rowb", [1, T], F32, offset=G5G)
    MROW0 = alloc("mrow0", [1, S], BF16, offset=G5G)
    WROW = place("wrow", [1, T], F32)
    ROWa = alloc("rowa", [1, T], F32, offset=G5G + 4096)
    MROW1 = alloc("mrow1", [1, S], BF16, offset=G5G + 4096)
    MROW = [MROW0, MROW1]
    WBC = place("wbc", [128, T], F32)
    ROWt = alloc("rowt", [1, T], F32, offset=G5G + 8192)
    SG = place("sg", [128, T], BF16)
    WRF = alloc("wrf", [128, 256], F32, offset=G5G + 12288)
    G5_END = pos[0]

    COS = alloc("cos_t", [128, T], F32, offset=aba_off)
    SIN = alloc("sin_t", [128, T], F32, offset=abb_off)
    M0 = place("m0_t", [128, 128], F32)
    M1 = place("m1_t", [128, 128], F32)
    IDB = place("idb", [128, 128], BF16)
    IDF = place("idf", [128, 128], F32)
    LNW = [place(f"lnw{i}", [128, NC], F32) for i in range(2)]
    RW2 = place("rw2", [128, 7 * NC], F32)
    ONESb = place("ones_b", [128, 1], BF16)
    ONESf = place("ones_f", [128, 1], F32)
    ONE1b = place("one1_b", [1, 128], BF16)
    ONE1f = place("one1_f", [1, 128], F32)
    ONERb = place("oner_b", [1, 512], BF16)
    BR16 = place("br16", [16, 1], F32)
    RB7 = place("rb7", [7, 1], F32)
    W7sb = P7sb
    S16 = place("s16", [16, 1], F32)
    TP128 = place("tp128", [128, 128], F32)
    SCQK = place("scqk", [128, 1], F32)
    TOK7 = place("tok7", [128, 16], F32)
    TOK7b = place("tok7b", [128, 16], F32)
    SCR = place("scr", [128, 16], F32)

    SBUF_LIMIT = 229340
    assert pos[0] <= SBUF_LIMIT, f"SBUF over budget: {pos[0]}"

    PSQ = nc.alloc_psum_tensor("psq", [128, 2048], F32)
    P4 = nc.alloc_psum_tensor("p4", [128, 512], F32)
    P5 = nc.alloc_psum_tensor("p5", [128, 512], F32)
    P6 = nc.alloc_psum_tensor("p6", [128, 1024], BF16)
    P7p = nc.alloc_psum_tensor("p7p", [128, 1024], BF16)

    def psq(i):
        return PSQ[:, i * 512:(i + 1) * 512]

    op, dma = em.op, em.dma

    def K(g, s="*"):
        return (g, s)

    tap_evs = []

    def tap(name, ap_fn, shape, dt, reads):
        if name not in taps:
            return
        t = nc.dram_tensor("tap_" + name, list(shape), dt,
                           kind="ExternalOutput").ap()
        ev = dma("SP", reads, [], "tapd",
                 lambda sy: sy.dma_start(out=t, in_=ap_fn()))
        tap_evs.append(ev)

    _pbreg = {}

    def get_pb2(sy):
        if "pb2" not in _pbreg:
            pid = nc.partition_id([ET.SP])
            _pbreg["pb2"] = nc.snap(pid // 2, [ET.SP], min_val=0, max_val=3)
        return _pbreg["pb2"]

    def load_consts():
        for nm2, dst, src in [("m0", M0, m0_in), ("m1", M1, m1_in),

                              ("idf", IDF, ident_f_in),
                              ("lnw0", LNW[0], Wb["ffn_ln1_w"]),
                              ("lnw1", LNW[1], Wb["ffn_ln2_w"]),
                              ("br16", BR16, Wb["attn_br"]),
                              ("rb7", RB7, Wb["ffn_router_b"])]:
            dma("SP", [], [K("G6", nm2)], "cload",
                lambda sy, d=dst, s=src: sy.dma_start(out=d[:], in_=s))
        dma("POOL", [], [K("G6", "idb")], "cload2",
            lambda g: g.dma_start(out=IDB[:], in_=ident_bf_in))
        for c in range(NC):
            dma("SP", [], [K("G6", "rw2")], "cload",
                lambda sy, c=c: sy.dma_start(
                    out=RW2[:, c * 7:(c + 1) * 7],
                    in_=W["ffn_router_w"][c * 128:(c + 1) * 128, :]))
        op("POOL", [], [K("G6", "tp128")],
           lambda g: g.memset(TP128[:], 0.0))
        for nm2, t, v in [("onesb", ONESb, 1.0), ("onesf", ONESf, 1.0),
                          ("one1b", ONE1b, 1.0), ("one1f", ONE1f, 1.0),
                          ("onerb", ONERb, 1.0), ("scqk", SCQK, SCALE_QK)]:
            op("POOL", [], [K("G6", nm2)],
               lambda g, tt=t, vv=v: g.memset(tt[:], vv))
        for c in range(NC):
            op("DVE", [K("G6", "rw2"), K("G6", "lnw1")], [K("G6", "rw2")],
               lambda v, c=c: v.tensor_scalar(
                   RW2[:, c * 7:(c + 1) * 7], RW2[:, c * 7:(c + 1) * 7],
                   LNW[1][:, c:c + 1], None, ALU.mult))
        for c in range(NC):
            dma("SP", [], [K("X", f"x{c}")], "xload",
                lambda sy, c=c: sy.dma_start(out=Xc[c][:], in_=x_in[c]))
        for e in ENGINES:
            em.barrier_tag(e, "cload")
            em.barrier_tag(e, "cload2")

    def load_tables(which):
        src_c = mla_cos_in if which == "mla" else sp_cos_in
        src_s = mla_sin_in if which == "mla" else sp_sin_in
        dma("SP", [], [K("WF", "aba")], "sl_tbl",
            lambda sy, s=src_c: sy.dma_start(out=COS[:], in_=s))
        dma("SP", [], [K("WF", "abb")], "sl_tbl",
            lambda sy, s=src_s: sy.dma_start(out=SIN[:], in_=s))

    wstate = {"n": 0}

    def wload(src2d, cols0, cols1, rows):
        i = wstate["n"] % NSLOT
        wstate["n"] += 1
        key = K("WS", f"{i}")
        nchunk = rows // 128
        ccols = cols1 - cols0
        dst = WS[i][:, 0:nchunk * ccols]
        src = src2d[:, cols0:cols1].rearrange("(n p) c -> p n c", p=128)
        dma("POOL", [], [key], f"sl_ws{i}",
            lambda g: g.dma_start(out=dst, in_=src))
        return WS[i], key, ccols

    def broadcast_row(row_sb, rkey, dst_sb, dkey):
        for h2 in range(2):
            ps = P4 if h2 == 0 else P5
            op("PE", [rkey, K("G6")], [K("P4" if h2 == 0 else "P5")],
               lambda t, h2=h2, ps=ps: t.matmul(
                   ps[:, 0:512], ONE1f[:],
                   row_sb[0:1, h2 * 512:(h2 + 1) * 512],
                   start=True, stop=True))
        op("ACT", [K("P4")], [dkey],
           lambda s: s.activation(dst_sb[:, 0:512], P4[:, 0:512], AF.Copy))
        op("ACT", [K("P5")], [dkey],
           lambda s: s.activation(dst_sb[:, 512:1024], P5[:, 0:512], AF.Copy))

    def stats_matmuls(src_getter, src_key_fn, want_mean):
        if src_getter is None:
            for c in range(NC):
                op("ACT", [K("X", f"x{c}")], [K("WF", "tba")],
                   lambda s, c=c: s.activation(TBa[:], Xc[c][:], AF.Copy))
                for h2, (ps, pk) in enumerate(((P4, "P4"), (psq(0), "PSQ"))):
                    op("PE", [K("WF", "tba"), K("G6", "onesb")], [K(pk)],
                       lambda t, c=c, h2=h2, ps=ps: t.matmul(
                           ps[0:1, 0:512], ONESb[:],
                           TBa[:, h2 * 512:(h2 + 1) * 512],
                           start=(c == 0), stop=(c == NC - 1)))
            return
        """Per chunk: (cast+)ones-matmuls. sum halves -> P4/PSQ0 rows;
        sumsq halves -> P5/PSQ1 rows. src_getter(c) returns bf16 [128,T] AP
        (already emitted); squares computed here into TBb."""
        for c in range(NC):
            if want_mean:
                for h2, (ps, pk) in enumerate(((P4, "P4"), (psq(0), "PSQ"))):
                    op("PE", [src_key_fn(c), K("G6")], [K(pk)],
                       lambda t, c=c, h2=h2, ps=ps: t.matmul(
                           ps[0:1, 0:512], ONESb[:],
                           src_getter(c)[:, h2 * 512:(h2 + 1) * 512],
                           start=(c == 0), stop=(c == NC - 1)))
            op("ACT", [src_key_fn(c)], [K("WF", "tbb")],
               lambda s, c=c: s.activation(TBb[:], src_getter(c)[:],
                                           AF.Square))
            for h2, (ps, pk) in enumerate(((P5, "P5"), (psq(1), "PSQ"))):
                op("PE", [K("WF", "tbb"), K("G6")], [K(pk)],
                   lambda t, c=c, h2=h2, ps=ps: t.matmul(
                       ps[0:1, 0:512], ONESb[:],
                       TBb[:, h2 * 512:(h2 + 1) * 512],
                       start=(c == 0), stop=(c == NC - 1)))

    def ln_rows_and_bc(with_mean):
        op("DVE", [K("P5")], [K("G5", "s2")],
           lambda v: v.tensor_scalar(ROWt[0:1, 0:512], P5[0:1, 0:512],
                                     1.0 / D, None, ALU.mult))
        op("DVE", [K("PSQ")], [K("G5", "s2")],
           lambda v: v.tensor_scalar(ROWt[0:1, 512:1024], psq(1)[0:1, 0:512],
                                     1.0 / D, None, ALU.mult))
        if with_mean:
            op("DVE", [K("P4")], [K("G5", "s1")],
               lambda v: v.tensor_scalar(ROWa[0:1, 0:512], P4[0:1, 0:512],
                                         1.0 / D, None, ALU.mult))
            op("DVE", [K("PSQ")], [K("G5", "s1")],
               lambda v: v.tensor_scalar(ROWa[0:1, 512:1024],
                                         psq(0)[0:1, 0:512],
                                         1.0 / D, None, ALU.mult))
            op("DVE", [K("G5", "s1")], [K("G5", "s0")],
               lambda v: v.tensor_mul(ROWb[0:1, :], ROWa[0:1, :],
                                      ROWa[0:1, :]))
            op("DVE", [K("G5", "s2"), K("G5", "s0")], [K("G5", "s2")],
               lambda v: v.tensor_sub(ROWt[0:1, :], ROWt[0:1, :],
                                      ROWb[0:1, :]))
        if stop_after == "rows1":
            return
        op("DVE", [K("G5", "s2")], [K("G5", "s2")],
           lambda v: v.tensor_scalar(ROWt[0:1, :], ROWt[0:1, :], EPS, None,
                                     ALU.add))
        op("ACT", [K("G5", "s2")], [K("G5", "s2")],
           lambda s: s.activation(ROWt[0:1, :], ROWt[0:1, :], AF.Sqrt))
        op("DVE", [K("G5", "s2")], [K("G5", "s0")],
           lambda v: v.reciprocal(ROWb[0:1, :], ROWt[0:1, :]))
        if stop_after == "rows2":
            return
        broadcast_row(ROWb, K("G5", "s0"), ABa, K("WF", "aba"))
        if stop_after == "bc1":
            return
        if with_mean:
            op("DVE", [K("G5", "s1"), K("G5", "s0")], [K("G5", "s1")],
               lambda v: v.tensor_mul(ROWa[0:1, :], ROWa[0:1, :],
                                      ROWb[0:1, :]))
            op("DVE", [K("G5", "s1")], [K("G5", "s1")],
               lambda v: v.tensor_scalar(ROWa[0:1, :], ROWa[0:1, :], -1.0,
                                         None, ALU.mult))
            broadcast_row(ROWa, K("G5", "s1"), ABb, K("WF", "abb"))
            if stop_after == "bc2":
                return

    def macaron_ln(with_hs):
        for c in range(NC):
            op("ACT", [K("X", f"x{c}")], [K("WF", "tba")],
               lambda s, c=c: s.activation(TBa[:], Xc[c][:], AF.Copy))
            for h2, (ps, pk) in enumerate(((P4, "P4"), (psq(0), "PSQ"))):
                op("PE", [K("WF", "tba"), K("G6")], [K(pk)],
                   lambda t, c=c, h2=h2, ps=ps: t.matmul(
                       ps[0:1, 0:512], ONESb[:],
                       TBa[:, h2 * 512:(h2 + 1) * 512],
                       start=(c == 0), stop=(c == NC - 1)))
            op("ACT", [K("X", f"x{c}")], [K("WF", "tbb")],
               lambda s, c=c: s.activation(TBb[:], Xc[c][:], AF.Square))
            for h2, (ps, pk) in enumerate(((P5, "P5"), (psq(1), "PSQ"))):
                op("PE", [K("WF", "tbb"), K("G6")], [K(pk)],
                   lambda t, c=c, h2=h2, ps=ps: t.matmul(
                       ps[0:1, 0:512], ONESb[:],
                       TBb[:, h2 * 512:(h2 + 1) * 512],
                       start=(c == 0), stop=(c == NC - 1)))
        ln_rows_and_bc(with_mean=True)
        if stop_after in ("rows1", "rows2", "bc1", "bc2"):
            return
        for c in range(NC):
            hkey = K("HS", f"hs{c}") if with_hs else K("HS", "*")
            hdst = HSc[c] if with_hs else HSc[c % 2]
            op("DVE", [K("X", f"x{c}"), K("WF", "aba")], [hkey],
               lambda v, c=c, hdst=hdst: v.tensor_mul(hdst[:], Xc[c][:],
                                                      ABa[:]))
            if stop_after == "ap1":
                continue
            if with_hs:
                op("DVE", [hkey, K("WF", "abb")], [hkey],
                   lambda v, c=c: v.tensor_add(HSc[c][:], HSc[c][:], ABb[:]))
                if stop_after == "ap2":
                    continue
                op("ACT", [hkey], [K("XN", f"{c}")],
                   lambda s, c=c: s.activation(XNc[c][:], HSc[c][:], AF.Copy))
            else:
                op("DVE", [hkey, K("WF", "abb")], [K("XN", f"{c}")],
                   lambda v, c=c, hdst=hdst: v.tensor_add(XNc[c][:], hdst[:],
                                                          ABb[:]))

    def rmsnorm(lnw_idx):
        stats_matmuls(lambda c: HSc[c], lambda c: K("HS", f"hs{c}"),
                      want_mean=False)
        ln_rows_and_bc(with_mean=False)
        if lnw_idx == 0:
            tap("rowb", lambda: ROWb[0:1, 0:T], (1, T), F32,
                [K("G5", "s0")])
            tap("aba", lambda: ABa[:, 0:T], (128, T), F32, [K("WF", "aba")])
        for c in range(NC):
            op("DVE", [K("WF", "aba"), K("G6")], [K("WF", "abb")],
               lambda v, c=c: v.tensor_scalar(ABb[:], ABa[:],
                                              LNW[lnw_idx][:, c:c + 1],
                                              None, ALU.mult))
            op("DVE", [K("HS", f"hs{c}"), K("WF", "abb")], [K("XN", f"{c}")],
               lambda v, c=c: v.tensor_mul(XNc[c][:], HSc[c][:], ABb[:]))

    def spill(buf_list, dram, grp, dramgrp):
        for c in range(NC):
            dma("SP", [K(grp, f"{'x' if grp == 'X' else 'hs'}{c}")],
                [K(dramgrp, c)], f"spill_{grp}",
                lambda sy, c=c: sy.dma_start(out=dram[c],
                                             in_=buf_list[c][:]))


    def mla_attention(cc_round):
        load_tables("mla")
        for name, dstl, dkey in (("ffn_kv_d_w", KVD, "kvd"),
                                 ("ffn_q_d_w", QD, "qd")):
            for lc in range(4):
                slot, skey, _ = wload(W[name], lc * 128, (lc + 1) * 128, D)
                for th in range(2):
                    ps, pk = (P4, "P4") if (lc * 2 + th) % 2 == 0 \
                        else (P5, "P5")
                    for c in range(NC):
                        op("PE", [skey, K("XN", f"{c}")], [K(pk)],
                           lambda t, slot=slot, c=c, th=th, ps=ps: t.matmul(
                               ps[:, 0:512], slot[:, c * 128:(c + 1) * 128],
                               XNc[c][:, th * 512:(th + 1) * 512],
                               start=(c == 0), stop=(c == NC - 1)))
                    op("ACT", [K(pk)], [K("X", f"{dkey}{lc}")],
                       lambda s, dstl=dstl, lc=lc, th=th, ps=ps:
                       s.activation(dstl[lc][:, th * 512:(th + 1) * 512],
                                    ps[:, 0:512], AF.Copy))
        if stop_after == "proj":
            return
        tap("xn0", lambda: XNc[0][:, 0:T], (128, T), BF16, [K("XN", "0")])
        tap("xn15", lambda: XNc[15][:, 0:T], (128, T), BF16,
            [K("XN", "15")])
        tap("cos", lambda: COS[:, 0:T], (128, T), F32, [K("WF", "aba")])
        tap("sin", lambda: SIN[:, 0:T], (128, T), F32, [K("WF", "abb")])
        tap("kvd0", lambda: KVD[0][:, 0:T], (128, T), BF16,
            [K("X", "kvd0")])
        tap("qd0", lambda: QD[0][:, 0:T], (128, T), BF16, [K("X", "qd0")])
        for hp in range(8):
            ha, hb = 2 * hp, 2 * hp + 1
            sa_, sb_ = K_sl[0], K_sl[1]
            ka, kb = K("X", "ksl0"), K("X", "ksl1")
            va, vb = Vt_sl[0], Vt_sl[1]
            vka, vkb = K("X", "vtsl0"), K("X", "vtsl1")
            slot, skey, _ = wload(W["ffn_k_u_w"], hp * 128, (hp + 1) * 128,
                                  LAT)
            for th in range(2):
                for lc in range(4):
                    op("PE", [skey, K("X", f"kvd{lc}")], [K("P4")],
                       lambda t, slot=slot, lc=lc, th=th: t.matmul(
                           P4[:, 0:512], slot[:, lc * 128:(lc + 1) * 128],
                           KVD[lc][:, th * 512:(th + 1) * 512],
                           start=(lc == 0), stop=(lc == 3)))
                op("ACT", [K("P4")], [ka],
                   lambda s, th=th, sa_=sa_: s.activation(
                       sa_[0:64, th * 512:(th + 1) * 512][:, :]
                       if False else sa_[0:64, th * 512:(th + 1) * 512],
                       P4[0:64, 0:512], AF.Copy))
                op("ACT", [K("P4")], [kb],
                   lambda s, th=th, sb_=sb_: s.activation(
                       sb_[0:64, th * 512:(th + 1) * 512],
                       P4[64:128, 0:512], AF.Copy))
            slot, skey, _ = wload(W["ffn_rope_k_w"], hp * 128,
                                  (hp + 1) * 128, D)
            for th in range(2):
                for c in range(NC):
                    op("PE", [skey, K("XN", f"{c}")], [K("P5")],
                       lambda t, slot=slot, c=c, th=th: t.matmul(
                           P5[:, 0:512], slot[:, c * 128:(c + 1) * 128],
                           XNc[c][:, th * 512:(th + 1) * 512],
                           start=(c == 0), stop=(c == NC - 1)))
                op("ACT", [K("P5")], [K("X", "rawa0")],
                   lambda s, th=th: s.activation(
                       RAWA[0][:, th * 512:(th + 1) * 512], P5[:, 0:512],
                       AF.Copy))
            if hp == 0:
                tap("krraw", lambda: RAWA[0][:, 0:T], (128, T), F32,
                    [K("X", "rawa0")])
            _rope(RAWA[0], K("X", "rawa0"), RAWA[1], K("X", "rawa1"),
                  quarter=True)
            op("DVE", [K("X", "rawa0"), K("X", "rawa1")], [ka],
               lambda v, sa_=sa_: v.tensor_add(sa_[64:128, 0:T],
                                               RAWA[0][0:64, :],
                                               RAWA[1][0:64, :]))
            op("DVE", [K("X", "rawa0"), K("X", "rawa1")], [kb],
               lambda v, sb_=sb_: v.tensor_add(sb_[64:128, 0:T],
                                               RAWA[0][64:128, :],
                                               RAWA[1][64:128, :]))
            slot, skey, _ = wload(W["ffn_v_u_w"], hp * 256, (hp + 1) * 256,
                                  LAT)
            for hh, (vd, vk) in enumerate(((va, vka), (vb, vkb))):
                for th in range(2):
                    for lc in range(4):
                        op("PE", [skey, K("X", f"kvd{lc}")], [K("P4")],
                           lambda t, slot=slot, lc=lc, hh=hh, th=th:
                           t.matmul(
                               P4[:, 0:512],
                               slot[:, lc * 256 + hh * 128:
                                    lc * 256 + hh * 128 + 128],
                               KVD[lc][:, th * 512:(th + 1) * 512],
                               start=(lc == 0), stop=(lc == 3)))
                    op("ACT", [K("P4")], [vk],
                       lambda s, vd=vd, th=th: s.activation(
                           vd[:, th * 512:(th + 1) * 512], P4[:, 0:512],
                           AF.Copy))
            for h, src, rk in ((ha, sa_, ka), (hb, sb_, kb)):
                dma("SP", [rk], [K("CCI")], "cci",
                    lambda sy, h=h, src=src: sy.dma_start(
                        out=cc_in[h * 128:(h + 1) * 128, :],
                        in_=src[:, 0:T]))
            for h, src, rk in ((ha, va, vka), (hb, vb, vkb)):
                dma("SP", [rk], [K("CCI")], "cci",
                    lambda sy, h=h, src=src: sy.dma_start(
                        out=cc_in[2048 + h * 128:2048 + (h + 1) * 128, :],
                        in_=src[:, 0:T]))
        if stop_after == "kv":
            return
        cc_ev = dma("POOL", [K("CCI")], [K("CCO")],
                    "ccol",
                    lambda g: g.collective_compute(
                        "AllGather", ALU.bypass,
                        replica_groups=[[0, 1, 2, 3, 4, 5, 6, 7]],
                        ins=[cc_in], outs=[cc_out]), inc=1)
        em.switch_tenant("XN")

        if stop_after == "coll":
            return

        def paircopy(sy):
            pb2 = get_pb2(sy)
            return sy.dma_start(out=kv_pair,
                                in_=cc_out[bass.ts(pb2, 8192)])
        dma("SP", [K("CCO")], [K("CCP")], "ccp", paircopy)
        if stop_after == "pc":
            return
        ccv = kv_pair.rearrange("(rank row) (tp i) -> rank row tp i",
                                rank=2, row=4096, tp=8, i=128)

        def gather(sy, dst, rowbase):
            src = ccv[:, rowbase:rowbase + 128]
            src = src.rearrange("r p tp i -> p tp r i")
            return sy.dma_start(out=dst[:, 0:S], in_=src)

        for h in range(NH):
            s = h % 2
            kk, vk = K("X", f"ksl{s}"), K("X", f"vtsl{s}")
            dma("SP", [K("CCP")], [kk], f"sl_k{s}",
                lambda sy, h=h: gather(sy, K_sl[h % 2], h * 128))
            dma("SP", [K("CCP")], [K("X", "prb0")], "sl_vf",
                lambda sy, h=h: gather(sy, VF, 2048 + h * 128))
            for t_ in range(NKT):
                pt, ptk = (P6, "P6") if (t_ % 16) < 8 else (P7p, "P7")
                col = (t_ % 8) * 128
                op("PE", [K("X", "prb0"), K("G6")], [K(ptk)],
                   lambda t, t_=t_, pt=pt, col=col: t.transpose(
                       pt[:, col:col + 128],
                       VF[:, t_ * 128:(t_ + 1) * 128], IDB[:]))
            for half, (pt, ptk) in enumerate(((P6, "P6"), (P7p, "P7"))):
                op("DVE", [K(ptk)], [vk],
                   lambda v, s=s, half=half, pt=pt: v.tensor_copy(
                       Vt_sl[s][:, half * 1024:(half + 1) * 1024],
                       pt[:, 0:1024]))
            if stop_after == "gath":
                continue
            qs = h % 2
            qkey = K("X", f"qsl{qs}")
            slot, skey, _ = wload(W["ffn_q_u_w"], h * 64, (h + 1) * 64, LAT)
            for th in range(2):
                for lc in range(4):
                    op("PE", [skey, K("X", f"qd{lc}")], [K("P4")],
                       lambda t, slot=slot, lc=lc, th=th: t.matmul(
                           P4[0:64, 0:512], slot[:, lc * 64:(lc + 1) * 64],
                           QD[lc][:, th * 512:(th + 1) * 512],
                           start=(lc == 0), stop=(lc == 3)))
                op("ACT", [K("P4")], [qkey],
                   lambda s_, th=th, qs=qs: s_.activation(
                       Q_sl[qs][0:64, th * 512:(th + 1) * 512],
                       P4[0:64, 0:512], AF.Copy, scale=SCQK[0:64, 0:1]))
            slot, skey, _ = wload(W["ffn_rope_q_w"], h * 64, (h + 1) * 64,
                                  LAT)
            for th in range(2):
                for lc in range(4):
                    op("PE", [skey, K("X", f"qd{lc}")], [K("P4")],
                       lambda t, slot=slot, lc=lc, th=th: t.matmul(
                           P4[0:64, 0:512], slot[:, lc * 64:(lc + 1) * 64],
                           QD[lc][:, th * 512:(th + 1) * 512],
                           start=(lc == 0), stop=(lc == 3)))
                op("ACT", [K("P4")], [K("X", "rawa0")],
                   lambda s_, th=th: s_.activation(
                       RAWA[0][0:64, th * 512:(th + 1) * 512],
                       P4[0:64, 0:512], AF.Copy, scale=SCQK[0:64, 0:1]))
            _rope(RAWA[0], K("X", "rawa0"), RAWA[1], K("X", "rawa1"),
                  quarter=True, rows=64)
            op("DVE", [K("X", "rawa0"), K("X", "rawa1")], [qkey],
               lambda v, qs=qs: v.tensor_add(Q_sl[qs][64:128, 0:T],
                                             RAWA[0][0:64, :],
                                             RAWA[1][0:64, :]))
            if stop_after == "qprep":
                continue
            if h == 0:
                tap("k0", lambda: K_sl[0][:, 0:S], (128, S), BF16,
                    [K("X", "ksl0")])
                tap("v0", lambda: Vt_sl[0][:, 0:S], (128, S), BF16,
                    [K("X", "vtsl0")])
                tap("q0", lambda: Q_sl[0][:, 0:T], (128, T), BF16,
                    [K("X", "qsl0")])
            for j in range(NT):
                f = 2 * j + 2
                _attn_tile(h, j, f, Q_sl[qs], qkey, K_sl[s], kk, Vt_sl[s],
                           vk, causal=True, mask_bc=None, mkey=None)
                if h == 0 and j == 7:
                    tap("prb7", lambda: PRB[7 % 2][:, 0:S], (128, S), BF16,
                        [K("X", f"prb{7 % 2}")])
            if True:
                tap(f"ao{h}", lambda h=h: AOc[h][:, 0:T], (128, T), BF16,
                    [K("XN", f"{h}")])

    def _rope(raw, rkey, shuf, skey2, quarter, rows=128):
        step = 32 if quarter else 64
        ngrp = rows // (2 * step)
        for g_ in range(ngrp):
            b0 = g_ * 2 * step
            dma("SP", [rkey], [skey2], "sl_shuf",
                lambda sy, b0=b0: sy.dma_start(
                    out=shuf[b0:b0 + step, :], in_=raw[b0 + step:
                                                       b0 + 2 * step, :]))
            dma("SP", [rkey], [skey2], "sl_shuf",
                lambda sy, b0=b0: sy.dma_start(
                    out=shuf[b0 + step:b0 + 2 * step, :],
                    in_=raw[b0:b0 + step, :]))
        op("DVE", [rkey, K("WF", "aba")], [rkey],
           lambda v: v.tensor_mul(raw[0:rows, :], raw[0:rows, :],
                                  COS[0:rows, :]))
        op("DVE", [skey2, K("WF", "abb")], [skey2],
           lambda v: v.tensor_mul(shuf[0:rows, :], shuf[0:rows, :],
                                  SIN[0:rows, :]))

    def _attn_tile(h, j, f, Q, qkey, Kf, kkey, Vt, vkey, causal, mask_bc,
                   mkey):
        w = f * 128
        nkc = (w + 511) // 512
        pq = j % 2
        prb, pbk = PRB[pq], K("X", f"prb{pq}")
        c0, c1 = 2 * (j % 2), 2 * (j % 2) + 1
        for kc in range(nkc):
            cw = min(512, w - kc * 512)
            op("PE", [qkey, kkey], [K("PSQ")],
               lambda t, kc=kc, cw=cw, j=j: t.matmul(
                   psq(kc)[:, 0:cw], Q[:, j * 128:(j + 1) * 128],
                   Kf[:, kc * 512:kc * 512 + cw], start=True, stop=True))
        if causal:
            for mi, mt in ((0, M0), (1, M1)):
                col = (f - 2 + mi) * 128
                op("DVE", [K("G6")], [K("PSQ")],
                   lambda v, col=col, mt=mt: v.tensor_add(
                       PSQ[:, col:col + 128], PSQ[:, col:col + 128], mt[:]))
        op("DVE", [K("PSQ")], [K("G6", f"scr{c0}")],
           lambda v, w=w, c0=c0: v.reduce_max(
               SCR[:, c0:c0 + 1], PSQ[:, 0:w], axis=AX.X, negate=True))
        if causal:
            op("ACT", [K("PSQ"), K("G6", f"scr{c0}")],
               [pbk, K("G6", f"scs{c0}")],
               lambda s, w=w, c0=c0, prb=prb: s.activation(
                   prb[:, 0:w], PSQ[:, 0:w], AF.Exp, bias=SCR[:, c0:c0 + 1],
                   accum_out=SCR[:, c0 + 8:c0 + 9]))
        else:
            op("ACT", [K("PSQ"), K("G6", f"scr{c0}")], [pbk],
               lambda s, w=w, c0=c0, prb=prb: s.activation(
                   prb[:, 0:w], PSQ[:, 0:w], AF.Exp,
                   bias=SCR[:, c0:c0 + 1]))
            op("DVE", [pbk, mkey], [pbk, K("G6", f"scs{c0}")],
               lambda v, w=w, c0=c0, prb=prb: v.scalar_tensor_tensor(
                   prb[:, 0:w], prb[:, 0:w], 0.0, mask_bc[:, 0:w],
                   ALU.bypass, ALU.mult,
                   accum_out=SCR[:, c0 + 8:c0 + 9]))
        op("DVE", [K("G6", f"scs{c0}")], [K("G6", f"scrc{c0}")],
           lambda v, c0=c0: v.reciprocal(SCR[:, c0 + 4:c0 + 5],
                                         SCR[:, c0 + 8:c0 + 9]))
        op("DVE", [pbk, K("G6", f"scrc{c0}")], [pbk],
           lambda v, w=w, c0=c0, prb=prb: v.tensor_scalar(
               prb[:, 0:w], prb[:, 0:w], SCR[:, c0 + 4:c0 + 5], None,
               ALU.mult))
        for t_ in range(f):
            pt, ptk = (P6, "P6") if t_ < 8 else (P7p, "P7")
            col = (t_ % 8) * 128
            op("PE", [pbk, K("G6")], [K(ptk)],
               lambda t, t_=t_, pt=pt, col=col, prb=prb: t.transpose(
                   pt[:, col:col + 128], prb[:, t_ * 128:(t_ + 1) * 128],
                   IDB[:]))
        n6 = min(f, 8) * 128
        op("DVE", [K("P6")], [K("X", "prbt")],
           lambda v, n6=n6: v.tensor_copy(PRBT[:, 0:n6], P6[:, 0:n6]))
        if f > 8:
            n7 = (f - 8) * 128
            op("DVE", [K("P7")], [K("X", "prbt")],
               lambda v, n7=n7: v.tensor_copy(PRBT[:, 1024:1024 + n7],
                                              P7p[:, 0:n7]))
        for t_ in range(f):
            op("PE", [K("X", "prbt"), vkey], [K("P4")],
               lambda t, t_=t_: t.matmul(
                   P4[:, 0:128], Vt[:, t_ * 128:(t_ + 1) * 128],
                   PRBT[:, t_ * 128:(t_ + 1) * 128],
                   start=(t_ == 0), stop=(t_ == f - 1)))
        op("ACT", [K("P4")], [K("XN", f"{h}")],
           lambda s, h=h, j=j: s.activation(
               AOc[h][:, j * 128:(j + 1) * 128], P4[:, 0:128], AF.Copy))

    def out_proj2(wname, bias_row, evac):
        if bias_row is not None:
            dma("POOL", [], [K("G5", "s0")], "sl_bias",
                lambda g: g.dma_start(out=BIAS[:, 0:D], in_=Wb[bias_row]))
        for dc in range(NC):
            slot, skey, _ = wload(W[wname], dc * 128, (dc + 1) * 128, D)
            if dc == 0 and wname == "ffn_o_w":
                tap("owslot", lambda slot=slot: slot[:, 0:2048], (128, 2048),
                    BF16, [skey])
                tap("aow0", lambda: AOc[0][:, 0:T], (128, T), BF16,
                    [K("XN", "0")])
            for th in range(2):
                ps, pk = (P4, "P4") if (dc * 2 + th) % 2 == 0 else (P5, "P5")
                for hc in range(NC):
                    last = (hc == NC - 1) and bias_row is None
                    op("PE", [skey, K("XN", f"{hc}")], [K(pk)],
                       lambda t, slot=slot, hc=hc, th=th, ps=ps, last=last:
                       t.matmul(
                           ps[:, 0:512], slot[:, hc * 128:(hc + 1) * 128],
                           AOc[hc][:, th * 512:(th + 1) * 512],
                           start=(hc == 0), stop=last))
                if bias_row is not None:
                    op("PE", [K("G5", "s0"), K("G6")], [K(pk)],
                       lambda t, dc=dc, th=th, ps=ps: t.matmul(
                           ps[:, 0:512], BIAS[0:1, dc * 128:(dc + 1) * 128],
                           ONERb[0:1, 0:512],
                           start=False, stop=True))
                evac(dc, th, ps, pk)

    def ffn_call(cc_round):
        if stop_after == "consts":
            return
        if stop_after in ("collonly", "collonly2"):
            dma("POOL", [K("X", "x0")], [K("CCI")], "cci",
                lambda g: g.dma_start(out=cc_in[0:128, :], in_=Xc[0][:]))
            dma("POOL", [K("CCI")], [K("CCO")], "ccol",
                lambda g: g.collective_compute(
                    "AllGather", ALU.bypass,
                    replica_groups=[[0, 1, 2, 3, 4, 5, 6, 7]],
                    ins=[cc_in], outs=[cc_out]), inc=1)
            return
        if stop_after == "stats":
            stats_matmuls(None, None, want_mean=True)
            return
        macaron_ln(with_hs=True)
        if stop_after in ("ln", "lncheck", "rows1", "rows2", "bc1", "bc2", "ap1", "ap2"):
            return
        spill(Xc, x_spill, "X", "SPX")
        em.switch_tenant("X")
        rmsnorm(0)
        if stop_after == "rms":
            return
        mla_attention(cc_round)
        def evac_ow(dc, th, ps, pk):
            op("DVE", [K(pk), K("HS", f"hs{dc}")], [K("HS", f"hs{dc}")],
               lambda v, dc=dc, th=th, ps=ps: v.tensor_add(
                   HSc[dc][:, th * 512:(th + 1) * 512],
                   HSc[dc][:, th * 512:(th + 1) * 512], ps[:, 0:512]))
        if stop_after in ("mla", "mlacheck", "proj", "kv", "coll", "pc", "gath", "qprep"):
            return
        out_proj2("ffn_o_w", None, evac_ow)
        if stop_after == "owcheck":
            return
        em.switch_tenant("XN")
        rmsnorm(1)
        if stop_after == "ow":
            return
        _router()
        tap("w7", lambda: W7sb[0:7, 0:T], (7, T), F32, [K("G5", "s0")])
        tap("xn2", lambda: XNc[0][:, 0:T], (128, T), BF16, [K("XN", "0")])
        if stop_after == "router":
            return
        spill(HSc, hs_spill, "HS", "SPH")
        em.switch_tenant("HS")
        em.switch_tenant("X")
        _moe()
        _final_combine()

    def _router():
        for th in range(2):
            for c in range(NC):
                op("PE", [K("G6"), K("HS", f"hs{c}")], [K("P4")],
                   lambda t, c=c, th=th: t.matmul(
                       P4[0:7, 0:512], RW2[:, c * 7:(c + 1) * 7],
                       HSc[c][:, th * 512:(th + 1) * 512],
                       start=(c == 0), stop=(c == NC - 1)))
            op("DVE", [K("P4"), K("WF", "aba")], [K("G5", "s0")],
               lambda v, th=th: v.tensor_mul(
                   P7sb[0:7, th * 512:(th + 1) * 512], P4[0:7, 0:512],
                   ABa[0:7, th * 512:(th + 1) * 512]))
            op("DVE", [K("G5", "s0"), K("G6")], [K("G5", "s0")],
               lambda v, th=th: v.tensor_scalar(
                   P7sb[0:7, th * 512:(th + 1) * 512],
                   P7sb[0:7, th * 512:(th + 1) * 512], RB7[0:7, 0:1],
                   None, ALU.add))
        for tc in range(NT):
            op("DVE", [K("G5", "s0")], [K("G6", "tp128")],
               lambda v, tc=tc: v.tensor_copy(
                   TP128[0:7, 0:128], P7sb[0:7, tc * 128:(tc + 1) * 128]))
            op("PE", [K("G6", "tp128"), K("G6", "idf")], [K("P5")],
               lambda t: t.transpose(P5[:, 0:128], TP128[:], IDF[:]))
            op("ACT", [K("P5")], [K("G6", "tok7")],
               lambda s: s.activation(TOK7[:, 0:7], P5[:, 0:7],
                                      AF.Sigmoid))
            op("DVE", [K("G6", "tok7")], [K("G6", "scr15")],
               lambda v: v.reduce_max(SCR[:, 15:16], TOK7[:, 0:7],
                                      axis=AX.X))
            op("DVE", [K("G6", "tok7"), K("G6", "scr15")], [K("G6", "tok7b")],
               lambda v: v.tensor_scalar(TOK7b[:, 0:7], TOK7[:, 0:7],
                                         SCR[:, 15:16], None, ALU.is_equal))
            op("DVE", [K("G6", "tok7"), K("G6", "tok7b")], [K("G6", "tok7b")],
               lambda v: v.scalar_tensor_tensor(
                   TOK7b[:, 0:7], TOK7b[:, 0:7], -1e30, TOK7[:, 0:7],
                   ALU.mult, ALU.add))
            op("DVE", [K("G6", "tok7b")], [K("G6", "scr14")],
               lambda v: v.reduce_max(SCR[:, 14:15], TOK7b[:, 0:7],
                                      axis=AX.X))
            op("DVE", [K("G6", "scr14"), K("G6", "scr15")],
               [K("G6", "scr13")],
               lambda v: v.tensor_add(SCR[:, 13:14], SCR[:, 14:15],
                                      SCR[:, 15:16]))
            op("DVE", [K("G6", "scr13")], [K("G6", "scr13")],
               lambda v: v.reciprocal(SCR[:, 13:14], SCR[:, 13:14]))
            op("DVE", [K("G6", "scr13")], [K("G6", "scr13")],
               lambda v: v.tensor_scalar(SCR[:, 13:14], SCR[:, 13:14],
                                         1.0 / 16.0, None, ALU.mult))
            op("DVE", [K("G6", "tok7"), K("G6", "scr14")], [K("G6", "tok7b")],
               lambda v: v.tensor_scalar(TOK7b[:, 0:7], TOK7[:, 0:7],
                                         SCR[:, 14:15], None, ALU.is_ge))
            op("DVE", [K("G6", "tok7"), K("G6", "tok7b")], [K("G6", "tok7b")],
               lambda v: v.tensor_mul(TOK7b[:, 0:7], TOK7b[:, 0:7],
                                      TOK7[:, 0:7]))
            op("DVE", [K("G6", "tok7b"), K("G6", "scr13")],
               [K("G6", "tok7b")],
               lambda v: v.tensor_scalar(TOK7b[:, 0:7], TOK7b[:, 0:7],
                                         SCR[:, 13:14], None, ALU.mult))
            op("DVE", [K("G6", "tok7b")], [K("G6", "tp128")],
               lambda v: v.tensor_copy(TP128[:, 0:7], TOK7b[:, 0:7]))
            op("PE", [K("G6", "tp128"), K("G6", "idf")], [K("P5")],
               lambda t: t.transpose(P5[:, 0:128], TP128[:], IDF[:]))
            op("ACT", [K("P5")], [K("G5", "s0")],
               lambda s, tc=tc: s.activation(
                   W7sb[0:7, tc * 128:(tc + 1) * 128], P5[0:7, 0:128],
                   AF.Copy))

    def _moe():
        for e in range(8):
            if e == 0:
                gname, uname, dname = "ffn_sh_gate", "ffn_sh_up", \
                    "ffn_sh_down"
                gsrc = W[gname]
                usrc = W[uname]
                dsrc = W[dname]
                op("POOL", [], [K("G5", "s2")],
                   lambda g: g.memset(WBC[:], 1.0 / 16.0))
            else:
                gsrc = W["ffn_r_gate"][e - 1]
                usrc = W["ffn_r_up"][e - 1]
                dsrc = W["ffn_r_down"][e - 1]
                dma("SP", [K("G5", "s0")], [K("G5", "s1")], "sl_wrow",
                    lambda sy, e=e: sy.dma_start(out=WROW[0:1, :],
                                                 in_=W7sb[e - 1:e, :]))
                broadcast_row(WROW, K("G5", "s1"), WBC, K("G5", "s2"))
            s = e % 2
            for ic in range(8):
                gslot, gkey, _ = wload(gsrc, ic * 128, (ic + 1) * 128, D)
                for th in range(2):
                    pk = "PSQ"
                    ps = psq((ic * 2 + th) % 2)
                    for c in range(NC):
                        op("PE", [gkey, K("XN", f"{c}")], [K(pk)],
                           lambda t, gslot=gslot, c=c, th=th, ps=ps:
                           t.matmul(ps[:, 0:512],
                                    gslot[:, c * 128:(c + 1) * 128],
                                    XNc[c][:, th * 512:(th + 1) * 512],
                                    start=(c == 0), stop=(c == NC - 1)))
                    op("ACT", [K(pk)], [K("G5", "s3")],
                       lambda s_, th=th, ps=ps: s_.activation(
                           SG[:, th * 512:(th + 1) * 512], ps[:, 0:512],
                           AF.Silu))
                uslot, ukey, _ = wload(usrc, ic * 128, (ic + 1) * 128, D)
                for th in range(2):
                    ps = psq(2 + (ic * 2 + th) % 2)
                    for c in range(NC):
                        op("PE", [ukey, K("XN", f"{c}")], [K("PSQ")],
                           lambda t, uslot=uslot, c=c, th=th, ps=ps:
                           t.matmul(ps[:, 0:512],
                                    uslot[:, c * 128:(c + 1) * 128],
                                    XNc[c][:, th * 512:(th + 1) * 512],
                                    start=(c == 0), stop=(c == NC - 1)))
                    op("DVE", [K("PSQ"), K("G5", "s3")],
                       [K("HS", f"he{s}_{ic}")],
                       lambda v, s=s, ic=ic, th=th, ps=ps: v.tensor_mul(
                           HE[s][ic][:, th * 512:(th + 1) * 512],
                           ps[:, 0:512], SG[:, th * 512:(th + 1) * 512]))
                op("POOL", [K("HS", f"he{s}_{ic}"), K("G5", "s2")],
                   [K("HS", f"he{s}_{ic}")],
                   lambda g, s=s, ic=ic: g.tensor_mul(
                       HE[s][ic][:], HE[s][ic][:], WBC[:]))
                if ic == 0 and e <= 1:
                    tap(f"he_e{e}", lambda s=s, ic=ic: HE[s][ic][:, 0:T],
                        (128, T), BF16, [K("HS", f"he{s}_{ic}")])
                    tap(f"wbc_e{e}", lambda: WBC[:, 0:T], (128, T), F32,
                        [K("G5", "s2")])
            for dc in range(NC):
                dslot, dkey, _ = wload(dsrc, dc * 128, (dc + 1) * 128, I_FF)
                if e == 0 and dc == 0:
                    tap("wdslot", lambda dslot=dslot: dslot[:, 0:1024],
                        (128, 1024), BF16, [dkey])
                    for icx in range(8):
                        tap(f"hec{icx}", lambda s=s, icx=icx:
                            HE[s][icx][:, 0:T], (128, T), BF16,
                            [K("HS", f"he{s}_{icx}")])
                for th in range(2):
                    ps, pk = (P4, "P4") if (dc * 2 + th) % 2 == 0 \
                        else (P5, "P5")
                    for ic in range(8):
                        op("PE", [dkey, K("HS", f"he{s}_{ic}")], [K(pk)],
                           lambda t, dslot=dslot, ic=ic, th=th, ps=ps, s=s:
                           t.matmul(ps[:, 0:512],
                                    dslot[:, ic * 128:(ic + 1) * 128],
                                    HE[s][ic][:, th * 512:(th + 1) * 512],
                                    start=(ic == 0), stop=(ic == 7)))
                    if e == 0:
                        op("ACT", [K(pk)], [K("X", f"x{dc}")],
                           lambda s_, dc=dc, th=th, ps=ps: s_.activation(
                               Xc[dc][:, th * 512:(th + 1) * 512],
                               ps[:, 0:512], AF.Copy))
                        if dc == 0 and th == 1:
                            tap("acc_e0", lambda: Xc[0][:, 0:T], (128, T),
                                F32, [K("X", "x0")])
                    else:
                        op("DVE", [K(pk), K("X", f"x{dc}")],
                           [K("X", f"x{dc}")],
                           lambda v, dc=dc, th=th, ps=ps: v.tensor_add(
                               Xc[dc][:, th * 512:(th + 1) * 512],
                               Xc[dc][:, th * 512:(th + 1) * 512],
                               ps[:, 0:512]))

    def _final_combine():
        for dc in range(NC):
            for th in range(2):
                i = (dc * 2 + th) % 2
                hk, xk = K("HS", f"hrc{i}"), K("HS", f"xrc{i}")
                dma("SP", [K("SPH", dc)], [hk], f"sl_hrc{i}",
                    lambda sy, dc=dc, th=th, i=i: sy.dma_start(
                        out=HRC[i][:],
                        in_=hs_spill[dc][:, th * 512:(th + 1) * 512]))
                dma("SP", [K("SPX", dc)], [xk], f"sl_xrc{i}",
                    lambda sy, dc=dc, th=th, i=i: sy.dma_start(
                        out=XRC[i][:],
                        in_=x_spill[dc][:, th * 512:(th + 1) * 512]))
                op("DVE", [hk, K("X", f"x{dc}")], [K("X", f"x{dc}")],
                   lambda v, dc=dc, th=th, i=i: v.scalar_tensor_tensor(
                       Xc[dc][:, th * 512:(th + 1) * 512], HRC[i][:], 0.5,
                       Xc[dc][:, th * 512:(th + 1) * 512],
                       ALU.mult, ALU.add))
                op("DVE", [xk, K("X", f"x{dc}")], [K("X", f"x{dc}")],
                   lambda v, dc=dc, th=th, i=i: v.tensor_add(
                       Xc[dc][:, th * 512:(th + 1) * 512],
                       Xc[dc][:, th * 512:(th + 1) * 512], XRC[i][:]))

    def sparse_call(cc_round):
        em.switch_tenant("HS")
        macaron_ln(with_hs=False)
        _routing(cc_round)
        load_tables("sp")
        spill(Xc, x_spill, "X", "SPX")
        em.switch_tenant("X")
        dma("POOL", [], [K("G5", "s0")], "sl_bias",
            lambda g: g.dma_start(out=BIAS[:, 0:1024], in_=Wb["attn_bk"]))
        for kv in range(NKV):
            s = kv % 2
            kk = K("X", f"ksl{s}")
            slot, skey, _ = wload(W["attn_Wk"], kv * 128, (kv + 1) * 128, D)
            for th in range(2):
                ps, pk = (P4, "P4") if th == 0 else (P5, "P5")
                for c in range(NC):
                    op("PE", [skey, K("XN", f"{c}")], [K(pk)],
                       lambda t, slot=slot, c=c, th=th, ps=ps: t.matmul(
                           ps[:, 0:512], slot[:, c * 128:(c + 1) * 128],
                           XNc[c][:, th * 512:(th + 1) * 512],
                           start=(c == 0), stop=False))
                op("PE", [K("G5", "s0"), K("G6")], [K(pk)],
                   lambda t, kv=kv, th=th, ps=ps: t.matmul(
                       ps[:, 0:512], BIAS[0:1, kv * 128:(kv + 1) * 128],
                       ONERb[0:1, 0:512],
                       start=False, stop=True))
                op("ACT", [K(pk)], [K("X", "rawa0")],
                   lambda s_, th=th, ps=ps: s_.activation(
                       RAWA[0][:, th * 512:(th + 1) * 512], ps[:, 0:512],
                       AF.Copy))
            _rope(RAWA[0], K("X", "rawa0"), RAWA[1], K("X", "rawa1"),
                  quarter=False)
            op("DVE", [K("X", "rawa0"), K("X", "rawa1")], [kk],
               lambda v, s=s: v.tensor_add(K_sl[s][:, 0:T], RAWA[0][:],
                                           RAWA[1][:]))
            dma("SP", [kk], [K("CCI")], "cci2",
                lambda sy, kv=kv, s=s: sy.dma_start(
                    out=cc2_in[kv * 128:(kv + 1) * 128, :],
                    in_=K_sl[s][:, 0:T]))
        dma("POOL", [], [K("G5", "s0")], "sl_bias",
            lambda g: g.dma_start(out=BIAS[:, 0:1024], in_=Wb["attn_bv"]))
        for kv in range(NKV):
            s = kv % 2
            vk = K("X", f"vtsl{s}")
            slot, skey, _ = wload(W["attn_Wv"], kv * 128, (kv + 1) * 128, D)
            for th in range(2):
                ps, pk = (P4, "P4") if th == 0 else (P5, "P5")
                for c in range(NC):
                    op("PE", [skey, K("XN", f"{c}")], [K(pk)],
                       lambda t, slot=slot, c=c, th=th, ps=ps: t.matmul(
                           ps[:, 0:512], slot[:, c * 128:(c + 1) * 128],
                           XNc[c][:, th * 512:(th + 1) * 512],
                           start=(c == 0), stop=False))
                op("PE", [K("G5", "s0"), K("G6")], [K(pk)],
                   lambda t, kv=kv, th=th, ps=ps: t.matmul(
                       ps[:, 0:512], BIAS[0:1, kv * 128:(kv + 1) * 128],
                       ONERb[0:1, 0:512],
                       start=False, stop=True))
                op("ACT", [K(pk)], [vk],
                   lambda s_, s=s, th=th, ps=ps: s_.activation(
                       Vt_sl[s][:, th * 512:(th + 1) * 512], ps[:, 0:512],
                       AF.Copy))
            dma("SP", [vk], [K("CCI")], "cci2",
                lambda sy, kv=kv, s=s: sy.dma_start(
                    out=cc2_in[1024 + kv * 128:1024 + (kv + 1) * 128, :],
                    in_=Vt_sl[s][:, 0:T]))
        dma("POOL", [K("CCI")], [K("CCO")], "ccol",
            lambda g: g.collective_compute(
                "AllGather", ALU.bypass,
                replica_groups=[[0, 1, 2, 3, 4, 5, 6, 7]],
                ins=[cc2_in], outs=[cc2_out]), inc=1)
        dma("POOL", [], [K("G5", "s0")], "sl_bias",
            lambda g: g.dma_start(out=BIAS[:, 0:D], in_=Wb["attn_bq"]))
        for h in range(NH):
            qkey = K("HS", f"qall{h}")
            slot, skey, _ = wload(W["attn_Wq"], h * 128, (h + 1) * 128, D)
            for th in range(2):
                ps, pk = (P4, "P4") if th == 0 else (P5, "P5")
                for c in range(NC):
                    op("PE", [skey, K("XN", f"{c}")], [K(pk)],
                       lambda t, slot=slot, c=c, th=th, ps=ps: t.matmul(
                           ps[:, 0:512], slot[:, c * 128:(c + 1) * 128],
                           XNc[c][:, th * 512:(th + 1) * 512],
                           start=(c == 0), stop=False))
                op("PE", [K("G5", "s0"), K("G6")], [K(pk)],
                   lambda t, h=h, th=th, ps=ps: t.matmul(
                       ps[:, 0:512], BIAS[0:1, h * 128:(h + 1) * 128],
                       ONERb[0:1, 0:512],
                       start=False, stop=True))
                op("ACT", [K(pk)], [K("X", "rawa0")],
                   lambda s_, th=th, ps=ps: s_.activation(
                       RAWA[0][:, th * 512:(th + 1) * 512], ps[:, 0:512],
                       AF.Copy, scale=SCQK[:, 0:1]))
            _rope(RAWA[0], K("X", "rawa0"), RAWA[1], K("X", "rawa1"),
                  quarter=False)
            op("DVE", [K("X", "rawa0"), K("X", "rawa1")], [qkey],
               lambda v, h=h: v.tensor_add(QAll[h][:, 0:T], RAWA[0][:],
                                           RAWA[1][:]))
        _bisection(cc_round)
        em.switch_tenant("XN")

        def paircopy2(sy):
            pb2 = get_pb2(sy)
            return sy.dma_start(out=kv2_pair,
                                in_=cc2_out[bass.ts(pb2, 4096)])
        dma("SP", [K("CCO")], [K("CCP")], "ccp", paircopy2)
        ccv = kv2_pair.rearrange("(rank row) (tp i) -> rank row tp i",
                                 rank=2, row=2048, tp=8, i=128)

        def gather2(sy, dst, rowbase):
            src = ccv[:, rowbase:rowbase + 128]
            src = src.rearrange("r p tp i -> p tp r i")
            return sy.dma_start(out=dst[:, 0:S], in_=src)

        for h in range(NH):
            kv = h // 2
            s = kv % 2
            if h % 2 == 0:
                kk, vk = K("X", f"ksl{s}"), K("X", f"vtsl{s}")
                dma("SP", [K("CCP")], [kk], f"sl_k{s}",
                    lambda sy, kv=kv, s=s: gather2(sy, K_sl[s], kv * 128))
                dma("SP", [K("CCP")], [K("X", "prb0")], "sl_vf",
                    lambda sy, kv=kv: gather2(sy, VF, 1024 + kv * 128))
                for t_ in range(NKT):
                    pt, ptk = (P6, "P6") if t_ < 8 else (P7p, "P7")
                    col = (t_ % 8) * 128
                    op("PE", [K("X", "prb0"), K("G6")], [K(ptk)],
                       lambda t, t_=t_, pt=pt, col=col: t.transpose(
                           pt[:, col:col + 128],
                           VF[:, t_ * 128:(t_ + 1) * 128], IDB[:]))
                for half, (pt, ptk) in enumerate(((P6, "P6"),
                                                  (P7p, "P7"))):
                    op("DVE", [K(ptk)], [K("X", f"vtsl{s}")],
                       lambda v, s=s, half=half, pt=pt: v.tensor_copy(
                           Vt_sl[s][:, half * 1024:(half + 1) * 1024],
                           pt[:, 0:1024]))
            else:
                kk, vk = K("X", f"ksl{s}"), K("X", f"vtsl{s}")
            ms = h % 2
            mkey = K("HS", f"mbc{ms}")
            dma("SP", [K("MD")], [K("G5", f"s{ms}")],
                f"sl_mrow{ms}",
                lambda sy, h=h, ms=ms: sy.dma_start(out=MROW[ms][0:1, :],
                                                    in_=mask_dram[h:h + 1]))
            for q4 in range(4):
                ps, pk = (P4, "P4") if q4 % 2 == 0 else (P5, "P5")
                op("PE", [K("G5", f"s{ms}"), K("G6")], [K(pk)],
                   lambda t, q4=q4, ms=ms, ps=ps: t.matmul(
                       ps[:, 0:512], ONE1b[:],
                       MROW[ms][0:1, q4 * 512:(q4 + 1) * 512],
                       start=True, stop=True))
                op("ACT", [K(pk)], [mkey],
                   lambda s_, q4=q4, ms=ms, ps=ps: s_.activation(
                       MBC[ms][:, q4 * 512:(q4 + 1) * 512], ps[:, 0:512],
                       AF.Copy))
            for j in range(NT):
                _attn_tile(h, j, NKT, QAll[h], K("HS", f"qall{h}"),
                           K_sl[s], kk, Vt_sl[s], vk, causal=False,
                           mask_bc=MBC[ms], mkey=mkey)
        em.switch_tenant("X")

        def evac_wo(dc, th, ps, pk):
            i = (dc * 2 + th) % 2
            xk = K("HS", f"xrc{i}")
            dma("SP", [K("SPX", dc)], [xk], f"sl_xrc{i}",
                lambda sy, dc=dc, th=th, i=i: sy.dma_start(
                    out=XRC[i][:],
                    in_=x_spill[dc][:, th * 512:(th + 1) * 512]))
            op("DVE", [K(pk), xk], [K("X", f"x{dc}")],
               lambda v, dc=dc, th=th, i=i, ps=ps: v.tensor_add(
                   Xc[dc][:, th * 512:(th + 1) * 512], XRC[i][:],
                   ps[:, 0:512]))
        out_proj2("attn_Wo", "attn_bo", evac_wo)

    def _routing(cc_round):
        dma("SP", [], [K("G5", "s3")], "sl_wrf",
            lambda sy: sy.dma_start(
                out=WRF[:, 0:256],
                in_=W["attn_Wr"].rearrange("(n p) c -> p n c", p=128)))
        for c in range(NC):
            op("PE", [K("G5", "s3"), K("G6")], [K("P4")],
               lambda t, c=c: t.matmul(
                   P4[0:16, 0:1], WRF[:, c * 16:(c + 1) * 16], ONESf[:],
                   start=(c == 0), stop=(c == NC - 1)))
        op("ACT", [K("P4")], [K("G6", "s16")],
           lambda s: s.activation(S16[0:16, 0:1], P4[0:16, 0:1], AF.Copy))
        for th in range(2):
            for c in range(NC):
                op("PE", [K("G5", "s3"), K("X", f"x{c}")], [K("P5")],
                   lambda t, c=c, th=th: t.matmul(
                       P5[0:16, 0:512], WRF[:, c * 16:(c + 1) * 16],
                       Xc[c][:, th * 512:(th + 1) * 512],
                       start=(c == 0), stop=(c == NC - 1)))
            op("DVE", [K("P5"), K("WF", "aba")], [K("HS", "mbc0")],
               lambda v, th=th: v.tensor_mul(
                   M16[0:16, th * 512:(th + 1) * 512], P5[0:16, 0:512],
                   ABa[0:16, th * 512:(th + 1) * 512]))
            op("DVE", [K("WF", "abb"), K("G6", "s16")], [K("HS", "r16")],
               lambda v, th=th: v.tensor_scalar(
                   R16[:, 5120 + th * 512:5120 + (th + 1) * 512],
                   ABb[0:16, th * 512:(th + 1) * 512], S16[0:16, 0:1],
                   None, ALU.mult))
            op("DVE", [K("HS", "mbc0"), K("HS", "r16")], [K("HS", "mbc0")],
               lambda v, th=th: v.tensor_add(
                   M16[0:16, th * 512:(th + 1) * 512],
                   M16[0:16, th * 512:(th + 1) * 512],
                   R16[:, 5120 + th * 512:5120 + (th + 1) * 512]))
            op("DVE", [K("HS", "mbc0"), K("G6")], [K("HS", "mbc0")],
               lambda v, th=th: v.tensor_scalar(
                   M16[0:16, th * 512:(th + 1) * 512],
                   M16[0:16, th * 512:(th + 1) * 512], BR16[0:16, 0:1],
                   None, ALU.add))
        op("ACT", [K("HS", "mbc0")], [K("HS", "mbc0")],
           lambda s: s.activation(M16[0:16, :], M16[0:16, :], AF.Sigmoid))
        dma("SP", [K("HS", "mbc0")], [K("RTI")], "rti",
            lambda sy: sy.dma_start(out=rt_in, in_=M16[0:16, :]))
        dma("POOL", [K("RTI")], [K("RTO")], "ccol",
            lambda g: g.collective_compute(
                "AllGather", ALU.bypass,
                replica_groups=[[0, 1, 2, 3, 4, 5, 6, 7]],
                ins=[rt_in], outs=[rt_out]), inc=1)

    def _bisection(cc_round):
        def rpcopy(sy):
            pb2 = get_pb2(sy)
            return sy.dma_start(out=rt_pair,
                                in_=rt_out[bass.ts(pb2, 32)])
        dma("SP", [K("RTO")], [K("RTP")], "ccp", rpcopy)
        rv = rt_pair.rearrange("(rank h) (tp i) -> rank h tp i",
                               rank=2, h=16, tp=8, i=128)

        def gatherr(sy):
            src = rv.rearrange("r h tp i -> h tp r i")
            return sy.dma_start(out=R16[:, 1024:1024 + 2048], in_=src)

        RF = R16[:, 1024:3072]
        GE = R16[:, 3072:5120]
        LO = R16[:, 0:1]
        HI = R16[:, 1:2]
        MID = R16[:, 2:3]
        CNT = R16[:, 3:4]
        CND = R16[:, 4:5]
        T1 = R16[:, 5:6]
        rk = K("HS", "r16")
        dma("SP", [K("RTP")], [rk], "sl_r16", gatherr)
        op("POOL", [], [rk], lambda g: g.memset(LO, 0.0))
        op("POOL", [], [rk], lambda g: g.memset(HI, 1.001))
        for it in range(N_BISECT):
            op("DVE", [rk], [rk], lambda v: v.tensor_add(MID, LO, HI))
            op("DVE", [rk], [rk],
               lambda v: v.tensor_scalar(MID, MID, 0.5, None, ALU.mult))
            op("DVE", [rk], [rk],
               lambda v: v.tensor_scalar(GE, RF, MID[0:16, 0:1], None,
                                         ALU.is_ge))
            op("DVE", [rk], [rk],
               lambda v: v.reduce_sum(CNT, GE, axis=AX.X))
            op("DVE", [rk], [rk],
               lambda v: v.tensor_scalar(CND, CNT, TOPK_KEYS, None,
                                         ALU.is_ge))
            op("DVE", [rk], [rk], lambda v: v.tensor_sub(T1, MID, LO))
            op("DVE", [rk], [rk], lambda v: v.tensor_mul(T1, T1, CND))
            op("DVE", [rk], [rk], lambda v: v.tensor_add(LO, LO, T1))
            op("DVE", [rk], [rk], lambda v: v.tensor_sub(T1, HI, MID))
            op("DVE", [rk], [rk], lambda v: v.tensor_mul(T1, T1, CND))
            op("DVE", [rk], [rk], lambda v: v.tensor_add(HI, MID, T1))
        op("DVE", [rk], [rk],
           lambda v: v.tensor_scalar(GE, RF, LO[0:16, 0:1], None,
                                     ALU.is_ge))
        dma("POOL", [rk], [K("MD")], "mdw",
            lambda g: g.dma_start(out=mask_dram, in_=GE))

    if stop_after != "collonly2":
        load_consts()
    else:
        for c in range(NC):
            dma("SP", [], [K("X", f"x{c}")], "xload",
                lambda sy, c=c: sy.dma_start(out=Xc[c][:], in_=x_in[c]))
    if 1 in phases:
        ffn_call(0)
    if 2 in phases:
        sparse_call(1)
    if 3 in phases:
        ffn_call(2)

    st = []
    SRC = HSc if stop_after in ("lncheck", "rmscheck", "owcheck") else Xc
    SGRP = "HS" if stop_after in ("lncheck", "rmscheck", "owcheck") else "X"
    SPFX = "hs" if stop_after in ("lncheck", "rmscheck", "owcheck") else "x"
    for c in range(NC):
        st.append(dma("SP", [K(SGRP, f"{SPFX}{c}")], [], "store",
                      lambda sy, c=c: sy.dma_start(out=out[c],
                                                   in_=SRC[c][:])))
    for ev in st + tap_evs:
        em.wait("SP", ev)

    with nc.Block() as block:
        em.finalize(block)
    return nc


def _stage_inputs(inputs, core):
    b, p = core // 2, core % 2
    x = np.asarray(inputs["L0_attn_hidden_states"])[b]
    tiles = x.reshape(NKT, 128, D)[p::2]
    loc = tiles.reshape(T, D)
    xf = np.ascontiguousarray(loc.T.reshape(NC, 128, T), dtype=np.float32)

    mc, ms, sc, ss = _rope_tables(p)
    m0, m1 = _causal_masks(p)

    def g(k):
        return np.ascontiguousarray(np.asarray(inputs[k], dtype=np.float32))

    return {
        "x": xf,
        "mla_cos": mc, "mla_sin": ms, "sp_cos": sc, "sp_sin": ss,
        "m0": m0, "m1": m1,
        "ident_bf": np.eye(128, dtype=np.float32),
        "ident_f": np.eye(128, dtype=np.float32),
        "attn_Wq": g("L0_attn_Wq"), "attn_Wk": g("L0_attn_Wk"),
        "attn_Wv": g("L0_attn_Wv"), "attn_Wo": g("L0_attn_Wo"),
        "attn_Wr": g("L0_attn_Wr"),
        "attn_bq": g("L0_attn_bq").reshape(1, D),
        "attn_bk": g("L0_attn_bk").reshape(1, 1024),
        "attn_bv": g("L0_attn_bv").reshape(1, 1024),
        "attn_bo": g("L0_attn_bo").reshape(1, D),
        "attn_br": g("L0_attn_br").reshape(16, 1),
        "ffn_kv_d_w": g("L0_ffn_kv_d_w"), "ffn_q_d_w": g("L0_ffn_q_d_w"),
        "ffn_k_u_w": g("L0_ffn_k_u_w"), "ffn_v_u_w": g("L0_ffn_v_u_w"),
        "ffn_q_u_w": g("L0_ffn_q_u_w"),
        "ffn_rope_k_w": g("L0_ffn_rope_k_w"),
        "ffn_rope_q_w": g("L0_ffn_rope_q_w"), "ffn_o_w": g("L0_ffn_o_w"),
        "ffn_sh_gate": g("L0_ffn_sh_gate"), "ffn_sh_up": g("L0_ffn_sh_up"),
        "ffn_sh_down": g("L0_ffn_sh_down"),
        "ffn_r_gate": g("L0_ffn_r_gate"), "ffn_r_up": g("L0_ffn_r_up"),
        "ffn_r_down": g("L0_ffn_r_down"),
        "ffn_router_w": g("L0_ffn_router_w"),
        "ffn_router_b": g("L0_ffn_router_b").reshape(7, 1),
        "ffn_ln1_w": np.ascontiguousarray(
            g("L0_ffn_ln1_w").reshape(NC, 128).T),
        "ffn_ln2_w": np.ascontiguousarray(
            g("L0_ffn_ln2_w").reshape(NC, 128).T),
    }


def kernel(**inputs):
    nc = build_nc()
    in_maps = [_stage_inputs(inputs, c) for c in range(8)]
    res = run_bass_kernel_spmd(nc, in_maps, core_ids=list(range(8)))
    outarr = np.empty((B, S, D), np.float32)
    for c in range(8):
        b, p = c // 2, c % 2
        xf = res.results[c]["out"]
        loc = xf.reshape(D, T).T
        outarr[b].reshape(NKT, 128, D)[p::2] = loc.reshape(NT, 128, D)
    return outarr
